# version 07 (final)
# baseline (speedup 1.0000x reference)
import contextlib
import numpy as np
import ml_dtypes
import concourse.bass as bass
import concourse.mybir as mybir
from concourse.bass_utils import run_bass_kernel_spmd

F32 = mybir.dt.float32
BF16 = mybir.dt.bfloat16
U32 = mybir.dt.uint32
ALU = mybir.AluOpType
ACT = mybir.ActivationFunctionType
AXX = mybir.AxisListType.X

NCORE = 8
D = 2048
T = 1024
KC = 16
ET = 16
NT = 8
ESL = 256
HB = 512
NB = HB // 16
TCH = (0, 512)
WB = (1, 2, 3, 0)


class Sched:
    def __init__(self):
        self.label = {}
        self.cnt = {}
        self.emitting = False
        self.ecnt = {}

    def inc(self, sem, label, n=1):
        if not self.emitting:
            c = self.cnt.get(sem, 0) + n
            self.cnt[sem] = c
            key = (sem, label)
            assert key not in self.label, f"dup {key}"
            self.label[key] = c
        else:
            c = self.ecnt.get(sem, 0) + n
            self.ecnt[sem] = c
            assert self.label[(sem, label)] == c, (sem, label, c)
        return self.label[(sem, label)]

    def at(self, sem, label):
        return self.label[(sem, label)]


def build_program():
    nc = bass.Bass("TRN2", target_bir_lowering=False, detect_race_conditions=False)

    xT_h = nc.dram_tensor("xT", [D, T], F32, kind="ExternalInput").ap()
    xtok_h = nc.dram_tensor("xtok", [T, D], F32, kind="ExternalInput").ap()
    wsl_h = nc.dram_tensor("wsl", [4, D, ESL], F32, kind="ExternalInput").ap()
    wf_h = nc.dram_tensor("wf", [ESL, D], F32, kind="ExternalInput").ap()
    bqkvg_h = nc.dram_tensor("bqkvg", [128, 4 * ET], F32, kind="ExternalInput").ap()
    bffn_h = nc.dram_tensor("bffn", [128, ET], F32, kind="ExternalInput").ap()
    xts_h = nc.dram_tensor("xts", [128, 1], F32, kind="ExternalInput").ap()
    maskp_h = nc.dram_tensor("maskp", [128, 4], F32, kind="ExternalInput").ap()
    id32_h = nc.dram_tensor("id32", [128, 128], F32, kind="ExternalInput").ap()
    idbf_h = nc.dram_tensor("idbf", [128, 128], BF16, kind="ExternalInput").ap()
    out_h = nc.dram_tensor("outT", [D, T], F32, kind="ExternalOutput").ap()

    wsp_in = nc.dram_tensor("wsp_in", [4, 2, KC, 128, ESL], BF16)
    wspo_kv = nc.dram_tensor("wspo_kv", [NCORE, 2, 2, KC, 128, ESL], BF16)
    wspo_gq = nc.dram_tensor("wspo_gq", [NCORE, 2, 2, KC, 128, ESL], BF16)
    wdqt_in = nc.dram_tensor("wdqt_in", [D, ESL], BF16)
    wdqt_out = nc.dram_tensor("wdqt_out", [NCORE, D, ESL], BF16)
    xdq_scr = nc.dram_tensor("xdq_scr", [NT, 128, D], BF16)
    gmax_in = nc.dram_tensor("gmax_in", [1, 1], F32)
    gmax_out = nc.dram_tensor("gmax_out", [1, 1], F32)
    tot_in = nc.dram_tensor("tot_in", [128, ET], F32)
    tot_out = nc.dram_tensor("tot_out", [4 * 128, ET], F32)
    mr_scr = nc.dram_tensor("mr_scr", [16, 128], F32)


    es = contextlib.ExitStack()

    def sbt(name, shape, dt):
        return es.enter_context(nc.sbuf_tensor(name, shape, dt))

    kvbuf = sbt("kvbuf", [128, ET, T], F32)
    xnhiT = sbt("xnhiT", [128, KC, T], BF16)
    xnloT = sbt("xnloT", [128, KC, T], BF16)
    wwin = sbt("wwin", [128, 2, 2, KC, 128], BF16)
    w0t = sbt("w0t", [128, 2, D], F32)
    qA = sbt("qA", [128, 4, HB], F32)
    w0h = sbt("w0h", [128, 2, D], BF16)
    w0l = sbt("w0l", [128, D], BF16)
    kvs = sbt("kvs", [128, T], F32)
    sgt = sbt("sgt", [128, T], F32)
    mb = sbt("mb", [128, T], F32)
    rb = sbt("rb", [128, T], F32)
    qS = sbt("qS", [128, 8 * NB], F32)
    qU = sbt("qU", [128, NB], U32)
    bqkvg = sbt("bqkvg_s", [128, 4 * ET], F32)
    bffn = sbt("bffn_s", [128, ET], F32)
    xts = sbt("xts_s", [128, 1], F32)
    maskp = sbt("maskp_s", [128, 4], F32)
    id32 = sbt("id32_s", [128, 128], F32)
    idbf = sbt("idbf_s", [128, 128], BF16)
    ones_c = sbt("ones_c", [128, 1], F32)
    ones_r = sbt("ones_r", [1, 128], F32)
    mrtok = sbt("mrtok", [128, 16], F32)
    prefix = sbt("prefix", [128, ET], F32)
    tot = sbt("tot_s", [128, ET], F32)
    tot4 = sbt("tot4_s", [128, 4, ET], F32)
    scal = sbt("scal_s", [128, 8], F32)
    stage = sbt("stage_s", [16, 128], F32)

    pp = [es.enter_context(nc.psum_tensor(f"pp{i}", [128, 512], F32)) for i in range(8)]
    sems = {s: es.enter_context(nc.semaphore("s_" + s))
            for s in ["ld", "pe", "dv", "ac", "gd", "cc", "cs", "wfs", "xs",
                      "xsb", "xt", "xr", "wd", "x2", "gm", "qs", "wq"]}

    S = Sched()

    m_row = mb[0:1, :]
    srow = kvs[0:1, :]
    srow2 = sgt[0:1, :]
    xdqT = xnhiT
    wdqw = wwin[:].rearrange("p a b c d -> p (a b) c d")
    qxdq = w0h[:, 0, :]
    qxn = w0t[:, 0, :]
    q_au = qA[:, 0, :]
    q_sg = qA[:, 1, :]
    q_r2 = qA[:, 2, :]
    q_r3 = qA[:, 3, :]
    q_bm = qS[:, 0 * NB:1 * NB]
    q_bs = qS[:, 1 * NB:2 * NB]
    q_qq = qS[:, 2 * NB:3 * NB]
    q_den = qS[:, 3 * NB:4 * NB]
    gm_cell = qS[0:1, 4 * NB:4 * NB + 1]
    gm_cell2 = qS[0:1, 4 * NB + 1:4 * NB + 2]
    amax_col = qS[:, 5 * NB:5 * NB + 1]
    BCOL = {w: bqkvg[:, w * ET:(w + 1) * ET] for w in range(4)}

    def emit_quant(eng, src, dst, r6col, tscol):
        eng.tensor_reduce(q_bm[:, :], src.rearrange("p (b k) -> p b k", k=16),
                          AXX, ALU.max, apply_absolute_value=True)
        eng.drain()
        eng.tensor_scalar(q_bs[:, :], q_bm[:, :], r6col, 448.0, ALU.mult, ALU.min)
        eng.drain()
        eng.tensor_scalar(qU[:, :], q_bs[:, :].bitcast(U32), 0x7F800000, None,
                          ALU.bitwise_and)
        eng.drain()
        eng.tensor_scalar(q_qq[:, :], qU[:, :].bitcast(F32), 0.125, 2.0 ** -9,
                          ALU.mult, ALU.max)
        eng.drain()
        eng.reciprocal(q_den[:, :], q_qq[:, :])
        eng.drain()
        eng.tensor_tensor(q_bm[:, :], q_bs[:, :], q_den[:, :], ALU.mult)
        eng.drain()
        eng.tensor_scalar(q_bs[:, :], q_bm[:, :], 0.5, None, ALU.is_gt)
        for j in range(1, 16):
            op = ALU.is_ge if j % 2 == 1 else ALU.is_gt
            eng.drain()
            eng.scalar_tensor_tensor(q_bs[:, :], q_bm[:, :], j + 0.5, q_bs[:, :],
                                     op, ALU.add)
        eng.drain()
        eng.tensor_tensor(q_bs[:, :], q_bs[:, :], q_qq[:, :], ALU.mult)
        eng.drain()
        eng.tensor_scalar(q_den[:, :], q_bs[:, :], tscol, 1e-30, ALU.mult, ALU.max)
        eng.drain()
        eng.reciprocal(q_den[:, :], q_den[:, :])
        eng.drain()
        eng.tensor_tensor(q_au[:, :], src,
                          q_den[:, :].unsqueeze(2).broadcast_to([128, NB, 16]),
                          ALU.mult)
        eng.tensor_scalar(q_au[:, :], q_au[:, :], -6.0, 6.0, ALU.max, ALU.min)
        eng.tensor_scalar(q_sg[:, :], q_au[:, :], 0.0, None, ALU.is_ge)
        eng.tensor_scalar(q_sg[:, :], q_sg[:, :], 2.0, -1.0, ALU.mult, ALU.add)
        eng.scalar_tensor_tensor(q_au[:, :], q_au[:, :], -1.0, q_au[:, :],
                                 ALU.mult, ALU.max)
        eng.tensor_scalar(q_r2[:, :], q_au[:, :], 0.25, None, ALU.is_gt)
        eng.scalar_tensor_tensor(q_r2[:, :], q_au[:, :], 0.75, q_r2[:, :], ALU.is_ge, ALU.add)
        eng.scalar_tensor_tensor(q_r2[:, :], q_au[:, :], 1.25, q_r2[:, :], ALU.is_gt, ALU.add)
        eng.scalar_tensor_tensor(q_r2[:, :], q_au[:, :], 1.75, q_r2[:, :], ALU.is_ge, ALU.add)
        eng.tensor_scalar(q_r3[:, :], q_au[:, :], 2.5, None, ALU.is_gt)
        eng.scalar_tensor_tensor(q_r3[:, :], q_au[:, :], 3.5, q_r3[:, :], ALU.is_ge, ALU.add)
        eng.scalar_tensor_tensor(q_r3[:, :], q_r3[:, :], 2.0, q_r2[:, :], ALU.mult, ALU.add)
        eng.tensor_scalar(q_r2[:, :], q_au[:, :], 5.0, 4.0, ALU.is_gt, ALU.mult)
        eng.tensor_tensor(q_r2[:, :], q_r2[:, :], q_r3[:, :], ALU.add)
        eng.tensor_tensor(q_r2[:, :], q_r2[:, :], q_sg[:, :], ALU.mult)
        eng.tensor_scalar(q_bs[:, :], q_bs[:, :], 0.5, None, ALU.mult)
        eng.drain()
        eng.tensor_tensor(dst, q_r2[:, :],
                          q_bs[:, :].unsqueeze(2).broadcast_to([128, NB, 16]),
                          ALU.mult)

    def stream_sync(eng):
        E = S.emitting

        def dma(label, out=None, in_=None, transpose=False, sem="ld"):
            if E:
                eng.dma_start(out=out, in_=in_, transpose=transpose).then_inc(sems[sem], 16)
            S.inc(sem, label, 16)

        def wait(sem, label):
            if E:
                eng.wait_ge(sems[sem], S.at(sem, label))

        for i, (dst, src) in enumerate([(bqkvg[:], bqkvg_h[:, :]), (bffn[:], bffn_h[:, :]),
                                        (xts[:], xts_h[:, :]), (maskp[:], maskp_h[:, :]),
                                        (id32[:], id32_h[:, :]), (idbf[:], idbf_h[:, :])]):
            dma(("c", i), dst, src, sem="cs")
        for i in range(2):
            dma(("wf", i), w0t[:, i, :], wf_h[i * 128:(i + 1) * 128, :], sem="wfs")
        for kc in range(KC):
            dma(("x", kc), kvbuf[:, kc, :], xT_h[kc * 128:(kc + 1) * 128, :],
                sem="xs" if kc < 8 else "xsb")
        for bi, w in enumerate(WB):
            for hf in range(2):
                if bi == 0 and hf == 0:
                    wait("dv", "wfamax_col")
                elif hf == 0:
                    wait("gd", ("wspb", bi - 1, 1, 1))
                else:
                    wait("dv", ("wsp", bi, 0))
                wslv = w0t[:].rearrange("p a (c e) -> p (a c) e", e=ESL)
                wsrc = wsl_h[w].rearrange("(c p) e -> c p e", p=128)
                if E:
                    for qq_ in range(2):
                        eng.dma_start(
                            out=wslv[:, qq_ * 4:(qq_ + 1) * 4, :],
                            in_=wsrc[hf * 8 + qq_ * 4:hf * 8 + (qq_ + 1) * 4, :, :]
                            .transpose([1, 0, 2])).then_inc(sems["ld"], 16)
                S.inc("ld", ("wsl", bi, hf), 32)
        wait("dv", ("wsp", 3, 1))
        for i in range(2):
            dma(("wf2", i), w0t[:, i, :], wf_h[i * 128:(i + 1) * 128, :], sem="wfs")
        for ph, agl, wout in (("p3", "ag_kv", wspo_kv), ("p5", "ag_gq", wspo_gq)):
            for et in range(ET):
                for hl in range(2):
                    if et == 0 and hl == 0:
                        wait("cc", agl)
                    if ph == "p3" and et > 0:
                        wait("pe", ("p3S", et - 1, hl))
                    elif ph == "p5":
                        wait("pe", ("p3S", ET - 1, hl) if et == 0 else ("p5S", et - 1, hl))
                    rank, sub = et // 2, et % 2
                    for wi in range(2):
                        if E:
                            for qq_ in range(4):
                                eng.dma_start(
                                    out=wwin[:, hl, wi, qq_ * 4:(qq_ + 1) * 4, :],
                                    in_=wout.ap()[rank, wi, hl, qq_ * 4:(qq_ + 1) * 4, :,
                                                  sub * 128:(sub + 1) * 128]
                                    .transpose([1, 0, 2])).then_inc(sems["ld"], 16)
                        S.inc("ld", (ph + "w", et, hl, wi), 64)
                if ph == "p3" and et % 2 == 1 and et // 2 < NT:
                    i = et // 2
                    if i == 0:
                        wait("dv", "mrtok")
                    else:
                        wait("dv", ("qdone", i - 1))
                    dma(("xtok", i), qxn, xtok_h[i * 128:(i + 1) * 128, :], sem="xt")
        for i in range(NT):
            if i == 0:
                wait("qs", ("qstash", NT - 1))
            if i >= 2:
                wait("pe", ("tpm", (i - 1) * KC - 1))
            dma(("xrb", i), w0h[:, i % 2, :], xdq_scr.ap()[i, :, :], sem="xr")
        for nt in range(ET):
            if nt == 0:
                wait("cc", "ag_wdq")
                wait("pe", ("p5S", ET - 1, 1))
            if nt >= 4:
                wait("pe", ("p2b", nt - 4, 1))
            rank, sub = nt // 2, nt % 2
            if E:
                for qq_ in range(4):
                    eng.dma_start(
                        out=wdqw[:, nt % 4, qq_ * 4:(qq_ + 1) * 4, :],
                        in_=wdqt_out.ap()[rank, qq_ * 512:(qq_ + 1) * 512,
                                          sub * 128:(sub + 1) * 128]
                        .rearrange("(c p) e -> p c e", p=128)).then_inc(sems["wd"], 16)
            S.inc("wd", ("wdqT", nt), 64)
            wait("dv", ("tp", NT * KC - 1) if nt < 2 else ("p2bout", nt - 2, 1))
            dma(("xT2", nt), w0t[:, nt % 2, 0:T], xT_h[nt * 128:(nt + 1) * 128, :],
                sem="x2")

    def stream_gp(eng):
        E = S.emitting

        def dma(label, out=None, in_=None, sem="gd"):
            if E:
                eng.dma_start(out=out, in_=in_).then_inc(sems[sem], 16)
            S.inc(sem, label, 16)

        def cc(label, kind, op, groups, ins, outs):
            if E:
                eng.collective_compute(kind, op, replica_groups=groups,
                                       ins=ins, outs=outs).then_inc(sems["cc"], 1)
            S.inc("cc", label, 1)

        def wait(sem, label):
            if E:
                eng.wait_ge(sems[sem], S.at(sem, label))

        wait("dv", "wfamax_col")
        dma("amax_row", stage[0:1, :], amax_col, sem="gm")
        wait("dv", "wfamax_one")
        dma("gmax_in", gmax_in.ap()[:, :], gm_cell, sem="gm")
        wait("gm", "gmax_in")
        cc("ar_gmax", "AllReduce", ALU.max, [list(range(NCORE))],
           [gmax_in.ap().opt()], [gmax_out.ap().opt()])
        wait("cc", "ar_gmax")
        dma("gmax_ld", gm_cell2, gmax_out.ap()[:, :], sem="gm")
        for bi in range(4):
            for hf in range(2):
                wait("dv", ("wsp", bi, hf))
                for hli, buf in ((0, w0h[:, 0, :]), (1, w0l[:])):
                    if E:
                        for qq_ in range(2):
                            eng.dma_start(
                                out=wsp_in.ap()[bi, hli, hf * 8 + qq_ * 4:hf * 8 + (qq_ + 1) * 4]
                                .rearrange("c p e -> p c e"),
                                in_=buf.rearrange("p (c e) -> p c e", e=ESL)[:, qq_ * 4:(qq_ + 1) * 4, :]
                            ).then_inc(sems["gd"], 16)
                    S.inc("gd", ("wspb", bi, hf, hli), 32)
            if bi == 1:
                wait("gd", ("wspb", 1, 1, 1))
                cc("ag_kv", "AllGather", ALU.bypass, [list(range(NCORE))],
                   [wsp_in.ap()[0:2].opt()], [wspo_kv.ap().opt()])
        wait("gd", ("wspb", 3, 1, 1))
        cc("ag_gq", "AllGather", ALU.bypass, [list(range(NCORE))],
           [wsp_in.ap()[2:4].opt()], [wspo_gq.ap().opt()])
        for j in range(32):
            wait("dv", ("wtev", j))
            i2, kc = j // KC, j % KC
            dma(("wdqtd", j), wdqt_in.ap()[kc * 128:(kc + 1) * 128, i2 * 128:(i2 + 1) * 128],
                w0l[:, (j % 8) * 256:(j % 8) * 256 + 128], sem="wq")
        wait("wq", ("wdqtd", 31))
        cc("ag_wdq", "AllGather", ALU.bypass, [list(range(NCORE))],
           [wdqt_in.ap().opt()], [wdqt_out.ap().opt()])
        for i in range(NT):
            wait("dv", ("qdone", i))
            dma(("qstash", i), xdq_scr.ap()[i, :, :], qxdq, sem="qs")
        wait("dv", "totdone")
        dma("totb", tot_in.ap()[:, :], tot[:, :])
        wait("gd", "totb")
        cc("ag_tot", "AllGather", ALU.bypass, [[0, 1, 2, 3], [4, 5, 6, 7]],
           [tot_in.ap().opt()], [tot_out.ap().opt()])
        wait("cc", "ag_tot")
        dma("tot4", tot4[:, :, :], tot_out.ap().rearrange("(r p) e -> p r e", p=128))
        for nt in range(ET):
            for tc in range(2):
                wait("dv", ("p2bout", nt, tc))
                dma(("out", nt, tc), out_h[nt * 128:(nt + 1) * 128, TCH[tc]:TCH[tc] + 512],
                    kvbuf[:, nt, TCH[tc]:TCH[tc] + 512])
        if E:
            eng.wait_ge(sems["gd"], S.at("gd", ("out", ET - 1, 1)))

    def stream_pe(eng):
        E = S.emitting

        def wait(sem, label):
            if E:
                eng.wait_ge(sems[sem], S.at(sem, label))

        def inc(label):
            if E:
                eng.drain().then_inc(sems["pe"], 1)
            S.inc("pe", label, 1)

        for kc in range(KC):
            wait("xs" if kc < 8 else "xsb", ("x", 7 if kc < 8 else 15))
            wait("ac", ("sq", kc))
            if kc == 0:
                wait("dv", "init")
            if E:
                for tc in range(2):
                    eng.matmul(pp[0 + tc][0:1, :], ones_c[:, 0:1],
                               kvbuf[:, kc, TCH[tc]:TCH[tc] + 512],
                               start=(kc == 0), stop=(kc == KC - 1))
                for tc in range(2):
                    eng.matmul(pp[2 + tc][0:1, :], ones_c[:, 0:1],
                               (kvs if kc % 2 == 0 else sgt)[:, TCH[tc]:TCH[tc] + 512],
                               start=(kc == 0), stop=(kc == KC - 1))
            inc(("p1s", kc))
        inc("p1stats")
        wait("gm", "gmax_ld")
        if E:
            eng.matmul(pp[4][:, 0:1], ones_r[0:1, :], gm_cell2, start=True, stop=True)
        inc("gmaxbc")
        wait("dv", "p1rows")
        wait("dv", "gmok")
        if E:
            for tc in range(2):
                eng.matmul(pp[4 + tc][:, :], ones_r[0:1, :],
                           m_row[0:1, TCH[tc]:TCH[tc] + 512], start=True, stop=True)
            for tc in range(2):
                eng.matmul(pp[6 + tc][:, :], ones_r[0:1, :],
                           srow[0:1, TCH[tc]:TCH[tc] + 512], start=True, stop=True)
        inc("p1bc")
        wait("dv", "p1rows")
        wait("dv", "rbdone")
        if E:
            for jj in range(8):
                eng.matmul(pp[0][:, jj:jj + 1], m_row[0:1, jj * 128:(jj + 1) * 128],
                           ones_r[0:1, 0:1], start=True, stop=True)
            for jj in range(8):
                eng.matmul(pp[0][:, 8 + jj:9 + jj], rb[0:1, jj * 128:(jj + 1) * 128],
                           ones_r[0:1, 0:1], start=True, stop=True)
        inc("mrtr")
        wait("dv", "wdq_done")
        wait("dv", "p1var")
        for j in range(32):
            i2, kc = j // KC, j % KC
            if j >= 2:
                wait("dv", ("wtev", j - 2))
            if E:
                bank = pp[2 + (j % 2)]
                slot = (j % 8) // 2
                eng.matmul(bank[:, slot * 128:slot * 128 + 128],
                           w0h[:, i2, kc * 128:(kc + 1) * 128], idbf[:, :],
                           start=True, stop=True)
            inc(("wdqtr", j))

        def proj(ph, prev_evB):
            for et in range(ET):
                base = 0 if et % 2 == 0 else 4
                if ph == "p3" and et == 0:
                    wait("dv", ("p1c", KC - 1))
                    wait("dv", ("wtev", 31))
                if et >= 2:
                    wait("dv", ("p3evB" if ph == "p3" else "p5evA", et - 2))
                elif prev_evB is not None:
                    wait("dv", (prev_evB, ET - 2 + et))
                for hl in range(2):
                    if hl == 0:
                        wait("ld", (ph + "w", et, 1, 1))
                    if E:
                        for kc in range(KC):
                            st = (hl == 0 and kc == 0)
                            for wi in range(2):
                                d0, d1 = pp[base + 2 * wi], pp[base + 2 * wi + 1]
                                wt = wwin[:, hl, wi, kc, :]
                                if hl == 0:
                                    eng.matmul(d0[:, :], wt, xnhiT[:, kc, 0:512], start=st, stop=False)
                                    eng.matmul(d1[:, :], wt, xnhiT[:, kc, 512:1024], start=st, stop=False)
                                    eng.matmul(d0[:, :], wt, xnloT[:, kc, 0:512], start=False, stop=False)
                                    eng.matmul(d1[:, :], wt, xnloT[:, kc, 512:1024], start=False, stop=False)
                                else:
                                    sp = (kc == KC - 1)
                                    eng.matmul(d0[:, :], wt, xnhiT[:, kc, 0:512], start=False, stop=sp)
                                    eng.matmul(d1[:, :], wt, xnhiT[:, kc, 512:1024], start=False, stop=sp)
                    inc((ph + "S", et, hl))

        proj("p3", None)
        proj("p5", "p3evB")

        tp = 0
        for i in range(NT):
            wait("xr", ("xrb", min(i + 1, NT - 1)))
            if i == 0:
                wait("dv", ("p5evA", ET - 1))
            for kc in range(KC):
                if tp >= 8:
                    wait("dv", ("tp", tp - 8))
                if E:
                    eng.matmul(pp[tp % 8][:, 0:128],
                               w0h[:, i % 2, kc * 128:(kc + 1) * 128],
                               idbf[:, :], start=True, stop=True)
                inc(("tpm", tp))
                tp += 1

        idx = 0
        for nt in range(ET):
            for tc in range(2):
                if idx == 0:
                    wait("dv", ("tp", NT * KC - 1))
                if idx >= 8:
                    wait("dv", ("p2bout", (idx - 8) // 2, (idx - 8) % 2))
                wait("wd", ("wdqT", min(nt + 2, ET - 1)))
                if E:
                    for kc in range(KC):
                        eng.matmul(pp[idx % 8][:, :], wdqw[:, nt % 4, kc, :],
                                   xdqT[:, kc, TCH[tc]:TCH[tc] + 512],
                                   start=(kc == 0), stop=(kc == KC - 1))
                inc(("p2b", nt, tc))
                idx += 1

    def stream_ac(eng):
        E = S.emitting

        def wait(sem, label):
            if E:
                eng.wait_ge(sems[sem], S.at(sem, label))

        def act(label, *args, **kw):
            if E:
                eng.activation(*args, **kw).then_inc(sems["ac"], 1)
            S.inc("ac", label, 1)

        for kc in range(KC):
            wait("xs" if kc < 8 else "xsb", ("x", 7 if kc < 8 else 15))
            if kc >= 2:
                wait("pe", ("p1s", kc - 2))
            act(("sq", kc), (kvs if kc % 2 == 0 else sgt)[:, :], kvbuf[:, kc, :],
                ACT.Square)
        wait("pe", "p1bc")
        if E:
            eng.activation(rb[:, 0:512], pp[6][:, :], ACT.Sqrt)
            eng.activation(rb[:, 512:1024], pp[7][:, :], ACT.Sqrt)
            eng.drain().then_inc(sems["ac"], 1)
        S.inc("ac", "sqrt", 1)
        for et in range(ET):
            base = 0 if et % 2 == 0 else 4
            wait("pe", ("p5S", et, 1))
            if et >= 1:
                wait("dv", ("p5evA", et - 1))
            act(("sig", et, 0), sgt[:, 0:512], pp[base + 0][:, :], ACT.Sigmoid,
                bias=BCOL[3][:, et:et + 1])
            act(("sig", et, 1), sgt[:, 512:1024], pp[base + 1][:, :], ACT.Sigmoid,
                bias=BCOL[3][:, et:et + 1])

    def stream_dv(eng):
        E = S.emitting

        def wait(sem, label):
            if E:
                eng.wait_ge(sems[sem], S.at(sem, label))

        def inc(label):
            if E:
                eng.drain().then_inc(sems["dv"], 1)
            S.inc("dv", label, 1)

        def incl(instr, label):
            if E:
                instr.then_inc(sems["dv"], 1)
            S.inc("dv", label, 1)

        if E:
            eng.wait_ge(sems["cs"], S.at("cs", ("c", 5)))
            eng.memset(ones_c[:], 1.0)
            eng.memset(ones_r[:], 1.0)
            eng.memset(scal[:, 3:4], 2688.0)
            eng.drain()
            eng.reciprocal(scal[:, 3:4], scal[:, 3:4])
            eng.drain()
            eng.tensor_scalar(scal[:, 0:1], xts[:, 0:1], 6.0, None, ALU.mult)
            eng.drain()
            eng.reciprocal(scal[:, 0:1], scal[:, 0:1])
            eng.drain()
        inc("init")

        wait("wfs", ("wf", 1))
        if E:
            eng.tensor_reduce(amax_col, w0t[:].rearrange("p a k -> p (a k)"),
                              AXX, ALU.max, apply_absolute_value=True)
        inc("wfamax_col")
        wait("gm", "amax_row")
        if E:
            eng.tensor_reduce(gm_cell, stage[0:1, :], AXX, ALU.max)
        inc("wfamax_one")

        for bi in range(4):
            for hf in range(2):
                wait("ld", ("wsl", bi, hf))
                if not (bi == 0 and hf == 0):
                    pb, ph_ = (bi, hf - 1) if hf == 1 else (bi - 1, 1)
                    wait("gd", ("wspb", pb, ph_, 1))
                if E:
                    eng.tensor_copy(w0h[:, 0, :], w0t[:, 0, :])
                    i2 = eng.tensor_tensor(w0l[:, :], w0t[:, 0, :], w0h[:, 0, :],
                                           ALU.subtract)
                incl(i2 if E else None, ("wsp", bi, hf))

        wait("pe", "gmaxbc")
        if E:
            eng.tensor_copy(scal[:, 4:5], pp[4][:, 0:1])
            eng.drain()
            eng.tensor_tensor(scal[:, 1:2], scal[:, 4:5], scal[:, 3:4], ALU.mult)
            eng.drain()
            eng.tensor_tensor(scal[:, 2:3], xts[:, 0:1], scal[:, 1:2], ALU.mult)
            eng.drain()
            eng.tensor_scalar(scal[:, 5:6], scal[:, 1:2], 6.0, None, ALU.mult)
            eng.drain()
            eng.reciprocal(scal[:, 5:6], scal[:, 5:6])
            eng.drain()
        inc("gmok")

        wait("wfs", ("wf2", 1))
        for i in range(2):
            for h in range(D // HB):
                if E:
                    emit_quant(eng, w0t[:, i, h * HB:(h + 1) * HB],
                               w0h[:, i, h * HB:(h + 1) * HB],
                               scal[:, 5:6], scal[:, 1:2])
        inc("wdq_done")

        wait("pe", "p1stats")
        if E:
            eng.tensor_scalar(m_row[0:1, 0:512], pp[0][0:1, :], 1.0 / 2048.0, None, ALU.mult)
            eng.tensor_scalar(m_row[0:1, 512:1024], pp[1][0:1, :], 1.0 / 2048.0, None, ALU.mult)
            eng.tensor_scalar(srow[0:1, 0:512], pp[2][0:1, :], 1.0 / 2048.0, None, ALU.mult)
            eng.tensor_scalar(srow[0:1, 512:1024], pp[3][0:1, :], 1.0 / 2048.0, None, ALU.mult)
            eng.tensor_tensor(srow2[0:1, :], m_row[0:1, :], m_row[0:1, :], ALU.mult)
            eng.tensor_tensor(srow[0:1, :], srow[0:1, :], srow2[0:1, :], ALU.subtract)
            eng.tensor_scalar(srow[0:1, :], srow[0:1, :], 1e-5, None, ALU.add)
            i2 = eng.drain()
        incl(i2 if E else None, "p1var")
        incl(eng.drain() if E else None, "p1rows")
        wait("ac", "sqrt")
        if E:
            eng.drain()
            eng.reciprocal(sgt[:, :], rb[:, :])
            eng.drain()
            eng.tensor_tensor(kvs[:, :], sgt[:, :], sgt[:, :], ALU.mult)
            eng.tensor_tensor(kvs[:, 0:512], kvs[:, 0:512], pp[6][:, :], ALU.mult)
            eng.tensor_tensor(kvs[:, 512:1024], kvs[:, 512:1024], pp[7][:, :], ALU.mult)
            eng.tensor_scalar(kvs[:, :], kvs[:, :], -0.5, 1.5, ALU.mult, ALU.add)
            eng.tensor_tensor(rb[:, :], sgt[:, :], kvs[:, :], ALU.mult)
            i2 = eng.drain()
        incl(i2 if E else None, "rbdone")
        wait("pe", "mrtr")
        if E:
            eng.tensor_copy(mb[:, 0:512], pp[4][:, :])
            eng.tensor_copy(mb[:, 512:1024], pp[5][:, :])
            i2 = eng.tensor_tensor(mb[:, :], mb[:, :], rb[:, :], ALU.mult)
        incl(i2 if E else None, "p1bcev")
        wait("pe", "mrtr")
        incl(eng.tensor_copy(mrtok[:, :], pp[0][:, 0:16]) if E else None, "mrtok")
        for j in range(32):
            wait("pe", ("wdqtr", j))
            if j >= 8:
                wait("wq", ("wdqtd", j - 1))
            if E:
                bank = pp[2 + (j % 2)]
                slot = (j % 8) // 2
                eng.tensor_copy(w0l[:, (j % 8) * 256:(j % 8) * 256 + 128],
                                bank[:, slot * 128:slot * 128 + 128])
            inc(("wtev", j))
        for kc in range(KC):
            if E:
                eng.tensor_tensor(kvs[:, :], kvbuf[:, kc, :], rb[:, :], ALU.mult)
                eng.tensor_tensor(sgt[:, :], kvs[:, :], mb[:, :], ALU.subtract)
                eng.tensor_copy(xnhiT[:, kc, :], sgt[:, :])
                i2 = eng.tensor_tensor(xnloT[:, kc, :], sgt[:, :], xnhiT[:, kc, :],
                                       ALU.subtract)
            incl(i2 if E else None, ("p1c", kc))
        if E:
            eng.memset(rb[:, :], 1.0)

        for et in range(ET):
            base = 0 if et % 2 == 0 else 4
            wait("pe", ("p3S", et, 1))
            if E:
                for tc in range(2):
                    eng.tensor_scalar(kvs[:, TCH[tc]:TCH[tc] + 512],
                                      pp[base + tc][:, :], BCOL[1][:, et:et + 1],
                                      None, ALU.add)
            inc(("p3evA", et))
            if E:
                for tc in range(2):
                    eng.scalar_tensor_tensor(kvbuf[:, et, TCH[tc]:TCH[tc] + 512],
                                             pp[base + 2 + tc][:, :],
                                             BCOL[2][:, et:et + 1],
                                             kvs[:, TCH[tc]:TCH[tc] + 512],
                                             ALU.add, ALU.mult)
                i2 = eng.tensor_reduce(tot[:, et:et + 1], kvbuf[:, et, :], AXX, ALU.add)
            incl(i2 if E else None, ("p3evB", et))
            for qq_ in (2 * et, 2 * et + 1):
                i, h = qq_ // (D // HB), qq_ % (D // HB)
                if h == 0:
                    wait("xt", ("xtok", i))
                    if i > 0:
                        wait("qs", ("qstash", i - 1))
                    if E:
                        eng.tensor_scalar(qxn[:, :], qxn[:, :], mrtok[:, i:i + 1],
                                          mrtok[:, 8 + i:9 + i], ALU.subtract, ALU.mult)
                        eng.drain()
                if E:
                    emit_quant(eng, qxn[:, h * HB:(h + 1) * HB],
                               qxdq[:, h * HB:(h + 1) * HB],
                               scal[:, 0:1], xts[:, 0:1])
                if h == D // HB - 1:
                    inc(("qdone", i))
        inc("totdone")

        wait("gd", "tot4")
        if E:
            eng.memset(prefix[:, :], 0.0)
            for r in range(4):
                eng.drain()
                i2 = eng.scalar_tensor_tensor(prefix[:, :], tot4[:, r, :],
                                              maskp[:, r:r + 1], prefix[:, :],
                                              ALU.mult, ALU.add)
        incl(i2 if E else None, "pfx")

        for et in range(ET):
            base = 0 if et % 2 == 0 else 4
            if E:
                i2 = eng.tensor_tensor_scan(kvs[:, :], rb[:, :], kvbuf[:, et, :],
                                            prefix[:, et:et + 1], ALU.mult, ALU.add)
            incl(i2 if E else None, ("scan", et))
            wait("ac", ("sig", et, 1))
            if E:
                for tc in range(2):
                    eng.scalar_tensor_tensor(mb[:, TCH[tc]:TCH[tc] + 512],
                                             pp[base + 2 + tc][:, :],
                                             BCOL[0][:, et:et + 1],
                                             kvs[:, TCH[tc]:TCH[tc] + 512],
                                             ALU.add, ALU.mult)
                    i2 = eng.tensor_tensor(kvbuf[:, et, TCH[tc]:TCH[tc] + 512],
                                           mb[:, TCH[tc]:TCH[tc] + 512],
                                           sgt[:, TCH[tc]:TCH[tc] + 512], ALU.mult)
            incl(i2 if E else None, ("p5evA", et))

        for tpi in range(NT * KC):
            wait("pe", ("tpm", tpi))
            i, kc = tpi // KC, tpi % KC
            incl(eng.tensor_copy(xdqT[:, kc, i * 128:(i + 1) * 128],
                                 pp[tpi % 8][:, 0:128]) if E else None, ("tp", tpi))

        idx = 0
        for nt in range(ET):
            for tc in range(2):
                wait("pe", ("p2b", nt, tc))
                if tc == 0:
                    wait("x2", ("xT2", min(nt + 1, ET - 1)))
                if E:
                    eng.tensor_scalar(kvs[:, 0:512], pp[idx % 8][:, :],
                                      scal[:, 2:3], bffn[:, nt:nt + 1],
                                      ALU.mult, ALU.add)
                    eng.tensor_tensor(kvs[:, 512:1024], kvs[:, 0:512],
                                      kvbuf[:, nt, TCH[tc]:TCH[tc] + 512], ALU.add)
                    i2 = eng.tensor_tensor(kvbuf[:, nt, TCH[tc]:TCH[tc] + 512],
                                           kvs[:, 512:1024],
                                           w0t[:, nt % 2, TCH[tc]:TCH[tc] + 512],
                                           ALU.add)
                incl(i2 if E else None, ("p2bout", nt, tc))
                idx += 1

    for fn in (stream_sync, stream_gp, stream_pe, stream_ac, stream_dv):
        fn(None)
    S.emitting = True
    block = es.enter_context(nc.Block())
    with es:
        @block.sync
        def _(eng):
            stream_sync(eng)

        @block.gpsimd
        def _(eng):
            stream_gp(eng)

        @block.tensor
        def _(eng):
            stream_pe(eng)

        @block.scalar
        def _(eng):
            stream_ac(eng)

        @block.vector
        def _(eng):
            stream_dv(eng)

    return nc


_NC_CACHE = {}


def kernel(L0_attn_x, L0_attn_Wq, L0_attn_bq, L0_attn_Wk, L0_attn_bk,
           L0_attn_Wv, L0_attn_bv, L0_attn_Wg, L0_attn_bg,
           L0_ffn_weight, L0_ffn_bias, L0_ffn_x_tensor_scale):
    x = np.ascontiguousarray(np.asarray(L0_attn_x, dtype=np.float32))
    Ws = [np.ascontiguousarray(np.asarray(W, dtype=np.float32).T)
          for W in (L0_attn_Wq, L0_attn_Wk, L0_attn_Wv, L0_attn_Wg)]
    wf = np.ascontiguousarray(np.asarray(L0_ffn_weight, dtype=np.float32))
    biases = [np.asarray(b, dtype=np.float32)
              for b in (L0_attn_bq, L0_attn_bk, L0_attn_bv, L0_attn_bg)]
    bffn = np.asarray(L0_ffn_bias, dtype=np.float32)
    xts = np.float32(L0_ffn_x_tensor_scale)

    bqkvg = np.ascontiguousarray(
        np.concatenate([b.reshape(ET, 128).T for b in biases], axis=1))
    bffn_t = np.ascontiguousarray(bffn.reshape(ET, 128).T)
    id32 = np.eye(128, dtype=np.float32)
    idbf = np.eye(128).astype(ml_dtypes.bfloat16)
    xts_b = np.full((128, 1), xts, dtype=np.float32)

    if "nc" not in _NC_CACHE:
        _NC_CACHE["nc"] = build_program()
    nc = _NC_CACHE["nc"]

    in_maps = []
    for c in range(NCORE):
        b, j = c // 4, c % 4
        xs = x[b, j * T:(j + 1) * T, :]
        wsl = np.stack([W[:, c * ESL:(c + 1) * ESL] for W in Ws], axis=0)
        mask = np.zeros((128, 4), dtype=np.float32)
        mask[:, :j] = 1.0
        in_maps.append({
            "xT": np.ascontiguousarray(xs.T),
            "xtok": np.ascontiguousarray(xs),
            "wsl": np.ascontiguousarray(wsl),
            "wf": np.ascontiguousarray(wf[c * ESL:(c + 1) * ESL, :]),
            "bqkvg": bqkvg,
            "bffn": bffn_t,
            "xts": xts_b,
            "maskp": mask,
            "id32": id32,
            "idbf": idbf,
        })

    res = run_bass_kernel_spmd(nc, in_maps, core_ids=list(range(NCORE)))
    kernel._last_res = res
    out = np.empty((2, 4096, D), dtype=np.float32)
    for c in range(NCORE):
        b, j = c // 4, c % 4
        out[b, j * T:(j + 1) * T, :] = res.results[c]["outT"].T
    return out
